# version 35
# baseline (speedup 1.0000x reference)
import numpy as np
import ml_dtypes
import concourse.bass as bass
import concourse.mybir as mybir
from concourse.bass_utils import run_bass_kernel_spmd
from contextlib import ExitStack

F32 = mybir.dt.float32
BF16 = mybir.dt.bfloat16
AF = mybir.ActivationFunctionType
ALU = mybir.AluOpType
AX = mybir.AxisListType

P = 128
TOK = 2048
NT = TOK // P
NG = 16
NF = 8
D = 1024
HID = 16384
EPS = 1e-5


class _Dummy:
    def then_inc(self, *a, **k):
        return self


class St:

    def __init__(self, record, sems, EV):
        self.record = record
        self.sems = sems
        self.EV = EV
        self.cnt = {k: 0 for k in sems}

    def bind_self(self, en, sem):
        self._self_en = en
        self._self_sem = sem

    def emit(self, f, sem=None, key=None, n=1):
        if not self.record and getattr(self, "_self_sem", None) is not None:
            if self.cnt[self._self_sem] > 0:
                self._self_en.wait_ge(self.sems[self._self_sem], self.cnt[self._self_sem])
        instr = _Dummy() if self.record else f()
        if sem is not None:
            self.cnt[sem] += n
            if key is not None:
                if self.record:
                    assert (sem, key) not in self.EV, (sem, key)
                    self.EV[(sem, key)] = self.cnt[sem]
                else:
                    assert self.EV[(sem, key)] == self.cnt[sem], (sem, key)
            instr.then_inc(self.sems[sem], n)
        return instr

    def wait(self, en, sem, key):
        if self.record:
            return
        val = key if isinstance(key, int) else self.EV[(sem, key)]
        en.wait_ge(self.sems[sem], val)

    def ev(self, sem, key):
        return self.EV[(sem, key)]


def build_nc():
    nc = bass.Bass("TRN2", target_bir_lowering=False)

    xo_h = nc.dram_tensor("xo", [TOK, D], F32, kind="ExternalInput")
    xt_h = nc.dram_tensor("xT", [D, TOK], F32, kind="ExternalInput")
    xst_h = nc.dram_tensor("xsT", [D, TOK], F32, kind="ExternalInput")
    wv_h = nc.dram_tensor("wv_t", [D, D], F32, kind="ExternalInput")
    wo_h = nc.dram_tensor("wo_t", [D, D], F32, kind="ExternalInput")
    mk_h = nc.dram_tensor("mk_t", [4, 64, 16, 16], F32, kind="ExternalInput")
    up_h = nc.dram_tensor("up_t", [4, 64, 16, 64], F32, kind="ExternalInput")
    wc_h = nc.dram_tensor("wc_t", [HID, D], F32, kind="ExternalInput")
    out_h = nc.dram_tensor("out", [TOK, D], F32, kind="ExternalOutput")
    mr_h = nc.dram_tensor("mrow_d", [4, TOK], F32)
    cd_h = nc.dram_tensor("Cd", [64, 16, 256], BF16)

    with ExitStack() as ctx:
        e = ctx.enter_context
        xn_t = e(nc.sbuf_tensor("xn_t", [P, HID], BF16))
        h_t = e(nc.sbuf_tensor("h_t", [P, HID], BF16))
        wc_bf = [e(nc.sbuf_tensor(f"wc_bf{i}", [P, 8 * D], BF16)) for i in range(2)]
        out_acc = e(nc.sbuf_tensor("out_acc", [P, HID], F32))
        cg = [e(nc.sbuf_tensor(f"cg{i}", [P, 1024], BF16)) for i in range(2)]
        sib = [e(nc.sbuf_tensor(f"sib{i}", [P, 1024], BF16)) for i in range(2)]
        x2b = [e(nc.sbuf_tensor(f"x2b{i}", [P, 1024], BF16)) for i in range(2)]
        tmp32 = e(nc.sbuf_tensor("tmp32", [P, 1024], F32))
        row_sb = e(nc.sbuf_tensor("row_sb", [1, 1024], F32))
        sib_rb = e(nc.sbuf_tensor("sib_rb", [P, TOK], BF16))
        mb_bc = e(nc.sbuf_tensor("mb_bc", [P, TOK], BF16))
        rb_bc = e(nc.sbuf_tensor("rb_bc", [P, TOK], BF16))
        csm = [e(nc.sbuf_tensor(f"csm{i}", [16, 512], BF16)) for i in range(2)]
        wpad = e(nc.sbuf_tensor("wpad", [P, 1024], BF16))
        ones_b = e(nc.sbuf_tensor("ones_b", [P, 1], BF16))
        ones_rc = e(nc.sbuf_tensor("ones_rc", [P, P], F32))
        eps_c = e(nc.sbuf_tensor("eps_c", [P, 4], F32))
        vsum_row = e(nc.sbuf_tensor("vsum_row", [1, D], F32))
        attn_row = e(nc.sbuf_tensor("attn_row", [1, D], F32))
        xsum_t = e(nc.sbuf_tensor("xsum_t", [P, 8], F32))
        xsb = e(nc.sbuf_tensor("xsb", [P, 16], F32))
        s2a = e(nc.sbuf_tensor("s2a", [1, 8], F32))
        s2c = e(nc.sbuf_tensor("s2c", [P, 1], F32))
        vsum_t = e(nc.sbuf_tensor("vsum_t", [P, 8], F32))
        xsum_bf = e(nc.sbuf_tensor("xsum_bf", [P, 8], BF16))
        vsum_bf = e(nc.sbuf_tensor("vsum_bf", [P, 8], BF16))
        ps_z = [e(nc.psum_tensor(f"ps_z{i}", [P, 1024], F32)) for i in range(2)]
        ps_o = [e(nc.psum_tensor(f"ps_o{i}", [P, 1024], F32)) for i in range(2)]
        sem_names = ["t", "v", "s", "ge", "qx0", "qx1", "qm", "qwv", "qwo", "qo",
                     "gu", "gx0", "gx1", "gb", "gb2", "sb0", "sb1", "sb2", "sb3",
                     "gc0", "gc1", "gc2", "gc3", "gw0", "gw1", "gs0", "gs1"]
        sems = {}
        for n_ in sem_names:
            sems[n_] = e(nc.semaphore("sem_" + n_))

        ut_s = out_acc[0:16, 0:HID]
        mt_t = nc.alloc_sbuf_tensor_at("mt_s", [16, 4096], F32,
                                       offset=nc.lookup_mloc(wc_bf[1]).addr)
        mt_s = mt_t[:, :]
        a0 = nc.lookup_mloc(sib[0]).addr
        assert nc.lookup_mloc(row_sb).addr == a0 + 12288
        assert nc.lookup_mloc(sib_rb).addr == a0 + 16384
        assert nc.lookup_mloc(wpad).addr == a0 + 30720
        wv_bf_t = nc.alloc_sbuf_tensor_at("wv_bf", [P, 8192], BF16, offset=a0)
        wv_bf = wv_bf_t[:, :]
        wo_bf_t = nc.alloc_sbuf_tensor_at("wo_bf", [P, 8192], BF16, offset=a0 + 16384)
        wo_bf = wo_bf_t[:, :]
        xr_t = nc.alloc_sbuf_tensor_at("xr_sb", [P, 8192], F32, offset=a0)
        xr_sb = xr_t[:, :]
        at32 = nc.lookup_mloc(tmp32).addr
        sib2_t = nc.alloc_sbuf_tensor_at("sib2", [P, 1024], BF16, offset=at32)
        sib3_t = nc.alloc_sbuf_tensor_at("sib3", [P, 1024], BF16, offset=at32 + 2048)
        sib4 = [sib[0], sib[1], sib2_t[:, :], sib3_t[:, :]]
        ax2 = nc.lookup_mloc(x2b[0]).addr
        assert nc.lookup_mloc(x2b[1]).addr == ax2 + 2048
        p2b_t = nc.alloc_sbuf_tensor_at("p2b", [P, 1024], F32, offset=ax2)
        p2 = [tmp32[:, :], p2b_t[:, :]]
        awp = nc.lookup_mloc(wpad).addr
        csm2_t = nc.alloc_sbuf_tensor_at("csm2", [16, 512], BF16, offset=awp)
        csm3_t = nc.alloc_sbuf_tensor_at("csm3", [16, 512], BF16, offset=awp + 1024)
        csm4 = [csm[0][:, :], csm[1][:, :], csm2_t[:, :], csm3_t[:, :]]
        ps_C = [ps_z[1][0:16, par * 512: par * 512 + 512] for par in range(2)]
        rowm = [ps_z[0][0:1, 0:1024], ps_z[0][0:1, 0:1024]]
        rowq = [ps_o[0][0:1, 0:1024], ps_o[1][0:1, 0:1024]]
        s_rowm4 = [ps_z[1][32:33, 0:1024], ps_z[1][0:1, 0:1024]]
        s_rowq4 = [ps_z[1][64:65, 0:1024], ps_z[1][64:65, 0:1024]]

        EV = {}

        def build_q(st, en):
            def dma(sem, dst, src, key):
                st.emit(lambda: en.dma_start(dst, src), sem=sem, key=key, n=16)

            for q in range(4):
                st.wait(en, "s", ("cevp", q))
                dma(f"gc{q % 4}", cd_h.ap()[2 * q:2 * q + 2].rearrange("rl i c -> i rl c"),
                    csm4[q % 4][:, :].rearrange("i (rl c) -> i rl c", rl=2), ("cdp", q))
            for h in range(2):
                st.wait(en, "v", ("rm", "own", h))
                dma("qm", mr_h.ap()[0:1, h * 1024:(h + 1) * 1024], row_sb[:, :], ("smr", h))
                st.wait(en, "v", ("rr", "own", h))
                dma("qm", mr_h.ap()[1:2, h * 1024:(h + 1) * 1024], row_sb[:, :], ("srr", "own", h))
            for q in range(4, 32):
                st.wait(en, "s", ("cevp", q))
                dma(f"gc{q % 4}", cd_h.ap()[2 * q:2 * q + 2].rearrange("rl i c -> i rl c"),
                    csm4[q % 4][:, :].rearrange("i (rl c) -> i rl c", rl=2), ("cdp", q))
            for h in range(2):
                st.wait(en, "v", ("rr", "sib", h))
                dma("qm", mr_h.ap()[2:3, h * 1024:(h + 1) * 1024], row_sb[:, :], ("srr", "sib", h))
            for i in range(16):
                h, ft = divmod(i, 8)
                if i == 0:
                    st.wait(en, "s", ("sbq", 15))
                    st.wait(en, "t", ("smm", 15))
                elif i == 1:
                    st.wait(en, "t", ("sqc", 15))
                else:
                    st.wait(en, "v", ("xsb", i - 2))
                dma(f"qx{i % 2}", p2[i % 2][:, :],
                    xst_h.ap()[ft * P:(ft + 1) * P, h * 1024:(h + 1) * 1024], ("sc2", i))
            st.wait(en, "t", ("tavrow",))
            for rg in range(4):
                if rg >= 2:
                    st.wait(en, "v", ("xadd", 4 * (rg - 2) + 3))
                dma(f"qx{rg % 2}",
                    xr_sb[:, (rg % 2) * 4096:(rg % 2) * 4096 + 4096].rearrange("p (b f) -> p b f", b=4),
                    xo_h.ap()[rg * 512:(rg + 1) * 512, :].rearrange("(b p) f -> p b f", p=P),
                    ("fx", rg))
            for og in range(4):
                st.wait(en, "v", ("spill", NG - 1, 4 * og + 3))
                dma("qo", out_h.ap()[og * 512:(og + 1) * 512, :].rearrange("(b p) f -> p b f", p=P),
                    out_acc[:, og * 4096:(og + 1) * 4096].rearrange("p (b f) -> p b f", b=4),
                    ("ox", og))
            st.wait(en, "qo", 64)

        def build_g(st, en):
            def dma(sem, dst, src, key):
                st.emit(lambda: en.dma_start(dst, src), sem=sem, key=key, n=16)

            dma("gx0", xn_t[:, 0:2048], xt_h.ap()[0:P, :], ("xT", 0))
            dma("gu", mt_s.rearrange("j (h r i) -> j h r i", h=4, r=64),
                mk_h.ap().rearrange("h r j i -> j h r i"), ("mk",))
            dma("gu", ut_s.rearrange("j (h r k) -> j h r k", h=4, r=64),
                up_h.ap().rearrange("h r j k -> j h r k"), ("up",))
            for ft in range(1, NF):
                if ft >= 2:
                    st.wait(en, f"gx{ft % 2}", ("xT", ft - 2))
                dma(f"gx{ft % 2}", xn_t[:, ft * 2048:(ft + 1) * 2048],
                    xt_h.ap()[ft * P:(ft + 1) * P, :], ("xT", ft))
            def sib_p1(i0, i1):
                for i in range(i0, i1):
                    h, ft = divmod(i, 8)
                    if i >= 4:
                        st.wait(en, "s", ("sbq", i - 4))
                        st.wait(en, "t", ("smm", i - 4))
                    dma(f"sb{i % 4}", sib4[i % 4][:, :],
                        xst_h.ap()[ft * P:(ft + 1) * P, h * 1024:(h + 1) * 1024], ("sc1", i))

            sib_p1(0, 4)
            st.wait(en, "qm", ("srr", "own", 1))
            st.emit(lambda: en.dma_start(
                mb_bc[:, :], bass.AP(mr_h, 0, [[0, P], [1, TOK]])), sem="gb", key=("mb",), n=16)
            st.emit(lambda: en.dma_start(
                rb_bc[:, :], bass.AP(mr_h, TOK, [[0, P], [1, TOK]])), sem="gb", key=("rb",), n=16)

            def load_wc(g):
                for hh in range(2):
                    src = wc_h.ap()[g * 1024 + hh * 512: g * 1024 + (hh + 1) * 512, :]
                    dma(f"gw{g % 2}",
                        wc_bf[g % 2][:, hh * 4096:(hh + 1) * 4096].rearrange("p (l of) -> p l of", l=4),
                        src.rearrange("(l p) of -> p l of", p=P), ("wc", g, hh))

            def load_cg(g):
                st.wait(en, f"gc{(2 * g) % 4}", ("cdp", 2 * g))
                st.wait(en, f"gc{(2 * g + 1) % 4}", ("cdp", 2 * g + 1))
                st.emit(lambda: en.memset(cg[g % 2][:, :], 0.0), sem="ge", key=("ms", g))
                st.wait(en, "ge", ("ms", g))
                for rl in range(4):
                    r = 4 * g + rl
                    rr = r % 8
                    dma(f"gs{g % 2}", cg[g % 2][16 * rr: 16 * rr + 16, rl * 256:(rl + 1) * 256],
                        cd_h.ap()[r], ("cs", g, rl))

            load_cg(0)
            load_cg(1)
            load_wc(0)
            sib_p1(4, 16)
            st.wait(en, "qm", ("srr", "sib", 1))
            st.emit(lambda: en.dma_start(
                sib_rb[:, :], bass.AP(mr_h, 2 * TOK, [[0, P], [1, TOK]])), sem="gb2", key=("sb_rb",), n=16)
            st.wait(en, "t", ("cmm", 63))
            load_wc(1)

            def load_attn_w(which):
                st.wait(en, "v", ("xsb", 15))
                st.wait(en, "t", ("sqc", 15))
                if which == "wv":
                    for half in range(2):
                        dma("qwv", wv_bf[:, half * 4096:(half + 1) * 4096].rearrange("p (c l) -> p c l", c=4),
                            wv_h.ap()[half * 512:(half + 1) * 512, :].rearrange("(c p) l -> p c l", p=P),
                            ("wv", half))
                else:
                    for qq in range(28, 32):
                        st.wait(en, f"gc{qq % 4}", ("cdp", qq))
                    for half in range(2):
                        dma("qwo", wo_bf[:, half * 4096:(half + 1) * 4096].rearrange("p (c l) -> p c l", c=4),
                            wo_h.ap()[half * 512:(half + 1) * 512, :].rearrange("(c p) l -> p c l", p=P),
                            ("wo", half))

            for g in range(2, NG):
                st.wait(en, "t", ("cp", g - 2, NT - 1))
                load_wc(g)
                st.wait(en, "t", ("z", g - 2, 15))
                load_cg(g)
                if g == 2:
                    load_attn_w("wv")
                elif g == 3:
                    load_attn_w("wo")

        def build_t(st, en):
            st.wait(en, "gu", ("up",))
            st.wait(en, "v", ("init",))
            rail_i = [8]

            def rail_fill(n):
                r1 = min(64, rail_i[0] + n)
                if r1 > rail_i[0]:
                    c_rails(rail_i[0], r1)
                    rail_i[0] = r1

            def own_stats():
              for h in range(2):
                for ft in range(NF):
                    st.wait(en, f"gx{ft % 2}", ("xT", ft))
                    if h == 1 and ft == 0:
                        st.wait(en, "v", ("rm", "own", 0))
                    for nn in range(2):
                        last = ft == 7 and nn == 1
                        st.emit(lambda h=h, ft=ft, nn=nn: en.matmul(
                            rowm[h][0:1, nn * 512:(nn + 1) * 512], ones_b[:, 0:1],
                            xn_t[:, ft * 2048 + h * 1024 + nn * 512: ft * 2048 + h * 1024 + (nn + 1) * 512],
                            start=(ft == 0), stop=(ft == 7)),
                            sem="t" if last else None, key=("mmh", h) if last else None)
              for h in range(2):
                for ft in range(NF):
                    i = h * 8 + ft
                    st.wait(en, "s", ("sq", i))
                    for nn in range(2):
                        st.emit(lambda h=h, i=i, nn=nn: en.matmul(
                            rowq[h][0:1, nn * 512:(nn + 1) * 512], ones_b[:, 0:1],
                            h_t[:, i * 1024 + nn * 512: i * 1024 + (nn + 1) * 512],
                            start=(i % 8 == 0), stop=(i % 8 == 7)),
                            sem="t" if nn == 1 else None, key=("qc", i) if nn == 1 else None)

            def c_rails(r0, r1):
                for r in range(r0, r1):
                    q = r // 2
                    if r % 2 == 0 and q >= 2:
                        st.wait(en, "s", ("cevp", q - 2))
                    for hh in range(4):
                        last = hh == 3
                        st.emit(lambda hh=hh, r=r, q=q: en.matmul(
                            ps_C[q % 2][:, (r % 2) * 256 + hh * 64: (r % 2) * 256 + (hh + 1) * 64],
                            mt_s[:, hh * 1024 + r * 16: hh * 1024 + (r + 1) * 16],
                            ut_s[:, hh * 4096 + r * 64: hh * 4096 + (r + 1) * 64],
                            start=True, stop=True),
                            sem="t" if last else None, key=("cmm", r) if last else None)

            def sib_mm(i):
                h = i // 8
                st.wait(en, f"sb{i % 4}", ("sc1", i))
                if i == 8:
                    st.wait(en, "s", ("cevp", 31))
                for nn in range(2):
                    st.emit(lambda h=h, i=i, nn=nn: en.matmul(
                        s_rowm4[h][0:1, nn * 512:(nn + 1) * 512], ones_b[:, 0:1],
                        sib4[i % 4][:, nn * 512:(nn + 1) * 512],
                        start=(i % 8 == 0), stop=(i % 8 == 7)),
                        sem="t" if nn == 1 else None, key=("smm", i) if nn == 1 else None)

            def sib_sq(i):
                h = i // 8
                st.wait(en, "s", ("sbq", i))
                if i == 8:
                    st.wait(en, "v", ("s2p", 0))
                for nn in range(2):
                    st.emit(lambda h=h, i=i, nn=nn: en.matmul(
                        s_rowq4[h][0:1, nn * 512:(nn + 1) * 512], ones_b[:, 0:1],
                        x2b[i % 2][:, nn * 512:(nn + 1) * 512],
                        start=(i % 8 == 0), stop=(i % 8 == 7)),
                        sem="t" if nn == 1 else None, key=("sqc", i) if nn == 1 else None)

            c_rails(0, 8)
            own_stats()
            rail_fill(64)
            def z_slot(g, i):
                return ps_z[0]

            def z_group(g, i):
                ft = g // 2
                tch, mc = divmod(i, 8)
                if i == 0:
                    st.wait(en, f"gs{g % 2}", ("cs", g, 3))
                    st.wait(en, "v", ("xnt", ft))
                if g == 0:
                    if i >= 1:
                        st.wait(en, "s", ("gelu", 0, i - 1))
                    else:
                        st.wait(en, "v", ("rm", "own", 1))
                elif i == 0:
                    st.wait(en, "s", ("gelu", g - 1, 15))
                else:
                    st.wait(en, "s", ("gelu", g, i - 1))
                for h2 in range(2):
                    last = h2 == 1
                    st.emit(lambda g=g, i=i, tch=tch, mc=mc, h2=h2, ft=ft: en.matmul(
                        z_slot(g, i)[:, h2 * 512:(h2 + 1) * 512],
                        cg[g % 2][:, mc * P:(mc + 1) * P],
                        xn_t[:, ft * 2048 + tch * 1024 + h2 * 512: ft * 2048 + tch * 1024 + (h2 + 1) * 512],
                        start=True, stop=True),
                        sem="t" if last else None, key=("z", g, i) if last else None)

            def attn_block():
                st.wait(en, "v", ("s2r",))
                st.emit(lambda: en.matmul(ps_z[1][0:P, 640:641], ones_rc[0:1, 0:P],
                                          s2a[0:1, 4:5], start=True, stop=True),
                        sem="t", key=("s2b",))
                st.wait(en, "qwv", ("wv", 1))
                st.wait(en, "qwo", ("wo", 1))
                st.wait(en, "v", ("xsbf",))
                for fc in range(8):
                    for half in range(2):
                        last = fc == 7 and half == 1
                        st.emit(lambda fc=fc, half=half: en.matmul(
                            ps_z[1][0:1, half * 512:(half + 1) * 512],
                            xsum_bf[:, fc:fc + 1],
                            wv_bf[:, fc * 1024 + half * 512: fc * 1024 + (half + 1) * 512],
                            start=(fc == 0), stop=(fc == 7)),
                            sem="t" if last else None, key=("tvrow",) if last else None)
                st.wait(en, "v", ("ev_vrow",))
                for j in range(8):
                    st.emit(lambda j=j: en.matmul(
                        ps_z[1][0:P, 512 + j:512 + j + 1], vsum_row[0:1, j * P:(j + 1) * P],
                        ones_rc[0:1, 0:1], start=True, stop=True),
                        sem="t" if j == 7 else None, key=("t8v",) if j == 7 else None)
                st.wait(en, "v", ("ev_vbf",))
                for fc in range(8):
                    for half in range(2):
                        last = fc == 7 and half == 1
                        st.emit(lambda fc=fc, half=half: en.matmul(
                            ps_z[1][0:1, half * 512:(half + 1) * 512],
                            vsum_bf[:, fc:fc + 1],
                            wo_bf[:, fc * 1024 + half * 512: fc * 1024 + (half + 1) * 512],
                            start=(fc == 0), stop=(fc == 7)),
                            sem="t" if last else None, key=("tavrow",) if last else None)
                st.wait(en, "v", ("ev_avrow",))
                for half in range(2):
                    st.emit(lambda half=half: en.matmul(
                        ps_z[1][:, half * 512:(half + 1) * 512],
                        ones_rc[0:1, 0:P], attn_row[0:1, half * 512:(half + 1) * 512],
                        start=True, stop=True),
                        sem="t" if half == 1 else None, key=("tbc",) if half == 1 else None)

            def cp_tile(g, tt):
                if tt == 0:
                    st.wait(en, "s", ("gelu", g, 7))
                if tt == 8:
                    st.wait(en, "s", ("gelu", g, 15))
                si = 16 * g + tt - 2
                if si >= 0:
                    st.wait(en, "v", ("spill", si // 16, si % 16))
                for ht in range(8):
                    for half in range(2):
                        last = ht == 7 and half == 1
                        st.emit(lambda g=g, tt=tt, ht=ht, half=half: en.matmul(
                            ps_o[tt % 2][:, half * 512:(half + 1) * 512],
                            h_t[:, ht * 2048 + tt * P: ht * 2048 + (tt + 1) * P],
                            wc_bf[g % 2][:, ht * 1024 + half * 512: ht * 1024 + (half + 1) * 512],
                            start=(ht == 0), stop=(ht == 7)),
                            sem="t" if last else None, key=("cp", g, tt) if last else None)

            st.wait(en, "gw0", ("wc", 0, 1))
            for i in range(16):
                z_group(0, i)
                sib_mm(i)
                if i < 8:
                    sib_sq(i)
                elif i >= 10:
                    sib_sq(i - 2)
                if i >= 8:
                    cp_tile(0, i - 8)
            for g in range(NG):
                if g == 3:
                    attn_block()
                if g > 0:
                    st.wait(en, f"gw{g % 2}", ("wc", g, 1))
                for tt in (range(8, NT) if g == 0 else range(NT)):
                    if tt < 8 and g > 0:
                        z_group(g, 8 + tt)
                    if tt >= 8 and g + 1 < NG:
                        z_group(g + 1, tt - 8)
                    if g == 0 and tt in (8, 9):
                        sib_sq(tt + 6)
                    cp_tile(g, tt)

        def build_s(st, en):
            st.bind_self(en, "s")

            def squares(kind, h, fill=None):
                for ft in range(NF):
                    i = h * 8 + ft
                    if kind == "own":
                        st.wait(en, f"gx{ft % 2}", ("xT", ft))
                        st.emit(lambda h=h, ft=ft, i=i: en.activation(
                            h_t[:, i * 1024:(i + 1) * 1024],
                            xn_t[:, ft * 2048 + h * 1024: ft * 2048 + (h + 1) * 1024],
                            AF.Square), sem="s", key=("sq", i))
                        if fill is not None:
                            fill(2)
                    else:
                        st.wait(en, f"sb{i % 4}", ("sc1", i))
                        if i >= 2:
                            st.wait(en, "t", ("sqc", i - 2))
                        else:
                            st.wait(en, "t", ("qc", 14 + i))
                        st.emit(lambda i=i: en.activation(
                            x2b[i % 2][:, :], sib4[i % 4][:, :],
                            AF.Square), sem="s", key=("sbq", i))

            def sbq_one(i):
                st.wait(en, f"sb{i % 4}", ("sc1", i))
                if i >= 2:
                    st.wait(en, "t", ("sqc", i - 2))
                else:
                    st.wait(en, "t", ("qc", 14 + i))
                st.emit(lambda i=i: en.activation(
                    x2b[i % 2][:, :], sib4[i % 4][:, :],
                    AF.Square), sem="s", key=("sbq", i))

            def sqrt_seg(kind, h, q_ps):
                st.wait(en, "v", ("var", kind, h))
                st.emit(lambda q_ps=q_ps: en.activation(
                    q_ps, q_ps, AF.Sqrt, bias=eps_c[0:1, 0:1]),
                    sem="s", key=("rsq", kind, h))

            def cevps(q0, q1):
                for q in range(q0, q1):
                    st.wait(en, "t", ("cmm", 2 * q + 1))
                    if q >= 4:
                        st.wait(en, f"gc{q % 4}", ("cdp", q - 4))
                    st.emit(lambda q=q: en.activation(
                        csm4[q % 4][:, :], ps_C[q % 2][:, :], AF.Identity),
                        sem="s", key=("cevp", q))

            cevps(0, 4)
            squares("own", 0)
            squares("own", 1)
            sqrt_seg("own", 0, rowq[0])
            sqrt_seg("own", 1, rowq[1])
            cevps(4, 32)
            for g in range(NG):
                for i in range(16):
                    tch, mc = divmod(i, 8)
                    st.wait(en, "t", ("z", g, i))
                    st.emit(lambda g=g, i=i, tch=tch, mc=mc: en.activation(
                        h_t[:, mc * 2048 + tch * 1024: mc * 2048 + (tch + 1) * 1024],
                        ps_z[0][:, :], AF.Gelu),
                        sem="s", key=("gelu", g, i))
                    if g == 0:
                        sbq_one(i)
                    if g == 0 and i == 9:
                        sqrt_seg("sib", 0, s_rowq4[0])
                    if g == 1 and i == 1:
                        sqrt_seg("sib", 1, s_rowq4[1])

        def build_v(st, en):
            st.bind_self(en, "v")

            def vself():
                st.wait(en, "v", st.cnt["v"] if not st.record else 0)

            st.emit(lambda: en.memset(ones_rc[:, :], 1.0))
            st.emit(lambda: en.memset(ones_b[:, :], 1.0))
            st.emit(lambda: en.memset(eps_c[:, :], EPS), sem="v", key=("init",))

            def seg(kind, h, m_ps, q_ps):
                if kind == "own" and h == 1:
                    st.wait(en, "qm", ("srr", "own", 0))
                if kind == "sib":
                    st.wait(en, "qm", ("srr", "own", 1) if h == 0 else ("srr", "sib", 0))
                st.wait(en, "t", ("mmh", h) if kind == "own" else ("smm", h * 8 + 7))
                st.emit(lambda m_ps=m_ps: en.tensor_scalar(
                    row_sb[:, :], m_ps, 1.0 / D, None, ALU.mult),
                    sem="v", key=("rm", kind, h))
                vself()
                st.emit(lambda: en.tensor_tensor(
                    attn_row[0:1, :], row_sb[:, :], row_sb[:, :], ALU.mult), sem="v")
                st.wait(en, "t", ("qc", h * 8 + 7) if kind == "own" else ("sqc", h * 8 + 7))
                vself()
                st.emit(lambda q_ps=q_ps: en.tensor_scalar(
                    q_ps, q_ps, 1.0 / D, None, ALU.mult), sem="v")
                vself()
                st.emit(lambda q_ps=q_ps: en.tensor_tensor(
                    q_ps, q_ps, attn_row[0:1, :], ALU.subtract), sem="v", key=("var", kind, h))
                st.wait(en, "s", ("rsq", kind, h))
                if kind == "own":
                    st.wait(en, "qm", ("smr", h))
                    st.emit(lambda q_ps=q_ps: en.tensor_copy(row_sb[:, :], q_ps),
                            sem="v", key=("rr", "own", h))
                else:
                    st.emit(lambda q_ps=q_ps, m_ps=m_ps: en.reciprocal(m_ps, q_ps), sem="v")
                    vself()
                    st.emit(lambda q_ps=q_ps, m_ps=m_ps: en.tensor_tensor(
                        q_ps, row_sb[:, :], m_ps, ALU.mult), sem="v")
                    vself()
                    st.emit(lambda q_ps=q_ps, h=h: en.reduce_sum(
                        s2a[0:1, h:h + 1], q_ps, axis=AX.X), sem="v", key=("s2p", h))
                    vself()
                    st.emit(lambda m_ps=m_ps: en.tensor_copy(row_sb[:, :], m_ps),
                            sem="v", key=("rr", "sib", h))

            seg("own", 0, rowm[0], rowq[0])
            seg("own", 1, rowm[1], rowq[1])
            st.wait(en, "gb", ("rb",))
            def _rbr():
                with nc.allow_low_precision(reason="rstd broadcast was already bf16"):
                    return en.reciprocal(rb_bc[:, :], rb_bc[:, :])
            st.emit(_rbr, sem="v", key=("rbr",))

            def norm(f0, f1):
                for ft in range(f0, f1):
                    st.wait(en, f"gx{ft % 2}", ("xT", ft))
                    st.emit(lambda ft=ft: en.tensor_tensor(
                        xn_t[:, ft * 2048:(ft + 1) * 2048], xn_t[:, ft * 2048:(ft + 1) * 2048],
                        mb_bc[:, :], ALU.subtract), sem="v")
                    st.emit(lambda ft=ft: en.tensor_tensor(
                        xn_t[:, ft * 2048:(ft + 1) * 2048], xn_t[:, ft * 2048:(ft + 1) * 2048],
                        rb_bc[:, :], ALU.mult), sem="v", key=("xnt", ft))
                    st.emit(lambda ft=ft: en.reduce_sum(
                        xsum_t[:, ft:ft + 1], xn_t[:, ft * 2048:(ft + 1) * 2048],
                        axis=AX.X), sem="v", key=("xso", ft))

            norm(0, 1)

            def p2_item(i):
                h, ft = divmod(i, 8)
                if i == 0:
                    st.wait(en, "gb2", ("sb_rb",))
                st.wait(en, f"qx{i % 2}", ("sc2", i))
                st.emit(lambda i=i, h=h: en.tensor_tensor(
                    p2[i % 2], p2[i % 2], sib_rb[:, h * 1024:(h + 1) * 1024],
                    ALU.mult), sem="v")
                vself()
                st.emit(lambda i=i: en.reduce_sum(
                    xsb[:, i:i + 1], p2[i % 2], axis=AX.X), sem="v", key=("xsb", i))

            def asm_item():
                vself()
                st.emit(lambda: en.tensor_tensor(xsum_t[:, :], xsum_t[:, :], xsb[:, 0:8], ALU.add), sem="v")
                vself()
                st.emit(lambda: en.tensor_tensor(xsum_t[:, :], xsum_t[:, :], xsb[:, 8:16], ALU.add), sem="v")
                st.emit(lambda: en.tensor_tensor(s2a[0:1, 4:5], s2a[0:1, 0:1], s2a[0:1, 1:2], ALU.add),
                        sem="v", key=("s2r",))

    
            def seg_sib(h):
                return lambda: seg("sib", h, s_rowm4[h], s_rowq4[h])
            norms = [(lambda ft=ft: norm(ft, ft + 1)) for ft in range(1, NF)]
            defer = [seg_sib(0), norms[0], norms[1], norms[2], seg_sib(1)]
            defer += norms[3:]
            defer += [(lambda i=i: p2_item(i)) for i in range(16)]
            defer.append(asm_item)

            def attn_evacs():
                st.wait(en, "t", ("s2b",))
                st.emit(lambda: en.tensor_copy(s2c[:, :], ps_z[1][0:P, 640:641]), sem="v", key=("s2c",))
                vself()
                st.emit(lambda: en.tensor_scalar(
                    xsum_t[:, :], xsum_t[:, :], s2c[:, 0:1], None, ALU.subtract),
                    sem="v", key=("xsc",))
                vself()
                st.emit(lambda: en.tensor_copy(xsum_bf[:, :], xsum_t[:, :]),
                        sem="v", key=("xsbf",))
                st.wait(en, "t", ("tvrow",))
                st.emit(lambda: en.tensor_copy(vsum_row[:, :], ps_z[1][0:1, 0:D]), sem="v", key=("ev_vrow",))
                st.wait(en, "t", ("t8v",))
                st.emit(lambda: en.tensor_copy(vsum_t[:, :], ps_z[1][0:P, 512:520]), sem="v", key=("ev_v",))
                vself()
                st.emit(lambda: en.tensor_copy(vsum_bf[:, :], vsum_t[:, :]),
                        sem="v", key=("ev_vbf",))
                st.wait(en, "t", ("tavrow",))
                st.emit(lambda: en.tensor_copy(attn_row[:, :], ps_z[1][0:1, 0:D]), sem="v", key=("ev_avrow",))

            st.wait(en, "t", ("cmm", 63))
            slot = 0
            for g in range(NG):
                if g == 3:
                    attn_evacs()
                for tt in range(NT):
                    st.wait(en, "t", ("cp", g, tt))
                    dst = out_acc[:, tt * 1024:(tt + 1) * 1024]
                    if g == 0:
                        st.emit(lambda dst=dst, tt=tt: en.tensor_copy(dst, ps_o[tt % 2][:, :]),
                                sem="v", key=("spill", g, tt))
                    else:
                        st.emit(lambda dst=dst, tt=tt: en.tensor_tensor(
                            dst, dst, ps_o[tt % 2][:, :], ALU.add),
                            sem="v", key=("spill", g, tt))
                    if defer and (slot >= 30 or slot % 2 == 0):
                        defer.pop(0)()
                    slot += 1
                    if g == 5 and tt < 8 or g == 6 and tt >= 8:
                        rg = tt // 4
                        st.wait(en, f"qx{rg % 2}", ("fx", rg))
                        xsrc = xr_sb[:, (rg % 2) * 4096 + (tt % 4) * 1024: (rg % 2) * 4096 + (tt % 4 + 1) * 1024]
                        st.emit(lambda dst=dst, xsrc=xsrc: en.tensor_tensor(
                            dst, dst, xsrc, ALU.add), sem="v", key=("xadd", tt))
                    if g == 8:
                        if tt == 0:
                            st.wait(en, "t", ("tbc",))
                        st.emit(lambda dst=dst: en.tensor_tensor(
                            dst, dst, ps_z[1][:, 0:D], ALU.add), sem="v", key=("aadd", tt))

        for b_ in (build_q, build_g, build_t, build_s, build_v):
            b_(St(True, sems, EV), None)
        with nc.Block() as block:
            @block.sync
            def _(sync):
                build_q(St(False, sems, EV), sync)

            @block.gpsimd
            def _(gpsimd):
                build_g(St(False, sems, EV), gpsimd)

            @block.tensor
            def _(tensor):
                build_t(St(False, sems, EV), tensor)

            @block.scalar
            def _(scalar):
                build_s(St(False, sems, EV), scalar)

            @block.vector
            def _(vector):
                build_v(St(False, sems, EV), vector)

    return nc


def kernel(L0_attn_x, L0_attn_w_qkv, L0_attn_w_o, L0_ffn_rail_maker,
           L0_ffn_rail_up_proj, L0_ffn_w_cproj):
    X = np.ascontiguousarray(np.asarray(L0_attn_x, dtype=np.float32)).reshape(16384, 1024)
    wqkv = np.asarray(L0_attn_w_qkv, dtype=np.float32)
    wv_t = np.ascontiguousarray(wqkv[2048:3072, :].T)
    wo_t = np.ascontiguousarray(np.asarray(L0_attn_w_o, dtype=np.float32).T)
    mk_t = np.ascontiguousarray(np.asarray(L0_ffn_rail_maker, dtype=np.float32).transpose(0, 1, 3, 2))
    up_t = np.ascontiguousarray(np.asarray(L0_ffn_rail_up_proj, dtype=np.float32).transpose(0, 1, 3, 2))
    wc = np.asarray(L0_ffn_w_cproj, dtype=np.float32)
    wc_t = np.ascontiguousarray(
        wc.reshape(1024, 4, 64, 64).transpose(2, 1, 3, 0).reshape(16384, 1024))

    XT = np.ascontiguousarray(X.T)

    nc = build_nc()
    in_maps = []
    for core in range(8):
        sib_ = core ^ 1
        in_maps.append({
            "xo": X[core * TOK:(core + 1) * TOK],
            "xT": np.ascontiguousarray(XT[:, core * TOK:(core + 1) * TOK]),
            "xsT": np.ascontiguousarray(XT[:, sib_ * TOK:(sib_ + 1) * TOK]),
            "wv_t": wv_t, "wo_t": wo_t, "mk_t": mk_t, "up_t": up_t,
            "wc_t": wc_t,
        })
    res = run_bass_kernel_spmd(nc, in_maps, core_ids=list(range(8)))
    out = np.concatenate([np.asarray(res.results[c]["out"]) for c in range(8)], axis=0)
    return out.reshape(4, 4096, 1024).astype(np.float32)


if __name__ == "__main__":
    rng = np.random.default_rng(0)
    ins = {
        "L0_attn_x": rng.standard_normal((4, 4096, 1024), dtype=np.float32),
        "L0_attn_w_qkv": (rng.standard_normal((3072, 1024), dtype=np.float32) * 0.02),
        "L0_attn_w_o": (rng.standard_normal((1024, 1024), dtype=np.float32) * 0.02),
        "L0_ffn_rail_maker": (rng.standard_normal((4, 64, 16, 16), dtype=np.float32) * 0.02),
        "L0_ffn_rail_up_proj": (rng.standard_normal((4, 64, 64, 16), dtype=np.float32) * 0.02),
        "L0_ffn_w_cproj": (rng.standard_normal((1024, 16384), dtype=np.float32) * 0.02),
    }
    o = kernel(**ins)
    print("out", o.shape, o.dtype, np.abs(o).max())


# version 36
# speedup vs baseline: 1.0322x; 1.0322x over previous
import numpy as np
import ml_dtypes
import concourse.bass as bass
import concourse.mybir as mybir
from concourse.bass_utils import run_bass_kernel_spmd
from contextlib import ExitStack

F32 = mybir.dt.float32
BF16 = mybir.dt.bfloat16
AF = mybir.ActivationFunctionType
ALU = mybir.AluOpType
AX = mybir.AxisListType

P = 128
TOK = 2048
NT = TOK // P
NG = 16
NF = 8
D = 1024
HID = 16384
EPS = 1e-5


class _Dummy:
    def then_inc(self, *a, **k):
        return self


class St:

    def __init__(self, record, sems, EV):
        self.record = record
        self.sems = sems
        self.EV = EV
        self.cnt = {k: 0 for k in sems}

    def bind_self(self, en, sem):
        self._self_en = en
        self._self_sem = sem

    def emit(self, f, sem=None, key=None, n=1):
        if not self.record and getattr(self, "_self_sem", None) is not None:
            if self.cnt[self._self_sem] > 0:
                self._self_en.wait_ge(self.sems[self._self_sem], self.cnt[self._self_sem])
        instr = _Dummy() if self.record else f()
        if sem is not None:
            self.cnt[sem] += n
            if key is not None:
                if self.record:
                    assert (sem, key) not in self.EV, (sem, key)
                    self.EV[(sem, key)] = self.cnt[sem]
                else:
                    assert self.EV[(sem, key)] == self.cnt[sem], (sem, key)
            instr.then_inc(self.sems[sem], n)
        return instr

    def wait(self, en, sem, key):
        if self.record:
            return
        val = key if isinstance(key, int) else self.EV[(sem, key)]
        en.wait_ge(self.sems[sem], val)

    def ev(self, sem, key):
        return self.EV[(sem, key)]


def build_nc():
    nc = bass.Bass("TRN2", target_bir_lowering=False)

    xo_h = nc.dram_tensor("xo", [TOK, D], F32, kind="ExternalInput")
    xt_h = nc.dram_tensor("xT", [D, TOK], F32, kind="ExternalInput")
    xst_h = nc.dram_tensor("xsT", [D, TOK], F32, kind="ExternalInput")
    wv_h = nc.dram_tensor("wv_t", [D, D], F32, kind="ExternalInput")
    wo_h = nc.dram_tensor("wo_t", [D, D], F32, kind="ExternalInput")
    mk_h = nc.dram_tensor("mk_t", [32, 4096], F32, kind="ExternalInput")
    up_h = nc.dram_tensor("up_t", [32, 8192], F32, kind="ExternalInput")
    wc_h = nc.dram_tensor("wc_t", [HID, D], F32, kind="ExternalInput")
    out_h = nc.dram_tensor("out", [TOK, D], F32, kind="ExternalOutput")
    mr_h = nc.dram_tensor("mrow_d", [4, TOK], F32)
    cd_h = nc.dram_tensor("Cd", [64, 16, 256], BF16)

    with ExitStack() as ctx:
        e = ctx.enter_context
        xn_t = e(nc.sbuf_tensor("xn_t", [P, HID], BF16))
        h_t = e(nc.sbuf_tensor("h_t", [P, HID], BF16))
        wc_bf = [e(nc.sbuf_tensor(f"wc_bf{i}", [P, 8 * D], BF16)) for i in range(2)]
        out_acc = e(nc.sbuf_tensor("out_acc", [P, HID], F32))
        cg = [e(nc.sbuf_tensor(f"cg{i}", [P, 1024], BF16)) for i in range(2)]
        sib = [e(nc.sbuf_tensor(f"sib{i}", [P, 1024], BF16)) for i in range(2)]
        x2b = [e(nc.sbuf_tensor(f"x2b{i}", [P, 1024], BF16)) for i in range(2)]
        tmp32 = e(nc.sbuf_tensor("tmp32", [P, 1024], F32))
        row_sb = e(nc.sbuf_tensor("row_sb", [1, 1024], F32))
        sib_rb = e(nc.sbuf_tensor("sib_rb", [P, TOK], BF16))
        mb_bc = e(nc.sbuf_tensor("mb_bc", [P, TOK], BF16))
        rb_bc = e(nc.sbuf_tensor("rb_bc", [P, TOK], BF16))
        csm = [e(nc.sbuf_tensor(f"csm{i}", [32, 256], BF16)) for i in range(2)]
        wpad = e(nc.sbuf_tensor("wpad", [P, 1536], BF16))
        ones_b = e(nc.sbuf_tensor("ones_b", [P, 1], BF16))
        ones_rc = e(nc.sbuf_tensor("ones_rc", [P, P], F32))
        eps_c = e(nc.sbuf_tensor("eps_c", [P, 4], F32))
        vsum_row = e(nc.sbuf_tensor("vsum_row", [1, D], F32))
        attn_row = e(nc.sbuf_tensor("attn_row", [1, D], F32))
        xsum_t = e(nc.sbuf_tensor("xsum_t", [P, 8], F32))
        xsb = e(nc.sbuf_tensor("xsb", [P, 16], F32))
        s2a = e(nc.sbuf_tensor("s2a", [1, 8], F32))
        s2c = e(nc.sbuf_tensor("s2c", [P, 1], F32))
        vsum_t = e(nc.sbuf_tensor("vsum_t", [P, 8], F32))
        xsum_bf = e(nc.sbuf_tensor("xsum_bf", [P, 8], BF16))
        vsum_bf = e(nc.sbuf_tensor("vsum_bf", [P, 8], BF16))
        ps_z = [e(nc.psum_tensor(f"ps_z{i}", [P, 1024], F32)) for i in range(2)]
        ps_o = [e(nc.psum_tensor(f"ps_o{i}", [P, 1024], F32)) for i in range(2)]
        sem_names = ["t", "v", "s", "ge", "qx0", "qx1", "qm", "qwv", "qwo", "qo",
                     "gu", "gx0", "gx1", "gb", "gb2", "sb0", "sb1", "sb2", "sb3",
                     "gc0", "gc1", "gc2", "gc3", "gw0", "gw1", "gs0", "gs1"]
        sems = {}
        for n_ in sem_names:
            sems[n_] = e(nc.semaphore("sem_" + n_))

        ut_s = out_acc[0:32, 0:8192]
        mt_t = nc.alloc_sbuf_tensor_at("mt_s", [32, 4096], F32,
                                       offset=nc.lookup_mloc(wc_bf[1]).addr)
        mt_s = mt_t[:, :]
        a0 = nc.lookup_mloc(sib[0]).addr
        assert nc.lookup_mloc(row_sb).addr == a0 + 12288
        assert nc.lookup_mloc(sib_rb).addr == a0 + 16384
        assert nc.lookup_mloc(wpad).addr == a0 + 29696
        wv_bf_t = nc.alloc_sbuf_tensor_at("wv_bf", [P, 8192], BF16, offset=a0)
        wv_bf = wv_bf_t[:, :]
        wo_bf_t = nc.alloc_sbuf_tensor_at("wo_bf", [P, 8192], BF16, offset=a0 + 16384)
        wo_bf = wo_bf_t[:, :]
        xr_t = nc.alloc_sbuf_tensor_at("xr_sb", [P, 8192], F32, offset=a0)
        xr_sb = xr_t[:, :]
        at32 = nc.lookup_mloc(tmp32).addr
        sib2_t = nc.alloc_sbuf_tensor_at("sib2", [P, 1024], BF16, offset=at32)
        sib3_t = nc.alloc_sbuf_tensor_at("sib3", [P, 1024], BF16, offset=at32 + 2048)
        sib4 = [sib[0], sib[1], sib2_t[:, :], sib3_t[:, :]]
        ax2 = nc.lookup_mloc(x2b[0]).addr
        assert nc.lookup_mloc(x2b[1]).addr == ax2 + 2048
        p2b_t = nc.alloc_sbuf_tensor_at("p2b", [P, 1024], F32, offset=ax2)
        p2 = [tmp32[:, :], p2b_t[:, :]]
        awp = nc.lookup_mloc(wpad).addr
        csm2_t = nc.alloc_sbuf_tensor_at("csm2", [32, 256], BF16, offset=awp)
        csm3_t = nc.alloc_sbuf_tensor_at("csm3", [32, 256], BF16, offset=awp + 1024)
        csm4 = [csm[0][:, :], csm[1][:, :], csm2_t[:, :], csm3_t[:, :]]
        ps_C = [ps_z[1][0:32, 0:256], ps_z[1][0:32, 512:768]]
        rowm = [ps_z[0][0:1, 0:1024], ps_z[0][0:1, 0:1024]]
        rowq = [ps_o[0][0:1, 0:1024], ps_o[1][0:1, 0:1024]]
        s_rowm4 = [ps_z[1][32:33, 0:1024], ps_z[1][0:1, 0:1024]]
        s_rowq4 = [ps_z[1][64:65, 0:1024], ps_z[1][64:65, 0:1024]]

        EV = {}

        def build_q(st, en):
            def dma(sem, dst, src, key):
                st.emit(lambda: en.dma_start(dst, src), sem=sem, key=key, n=16)

            for q in range(4):
                st.wait(en, "s", ("cevp", q))
                dma(f"gc{q % 4}", cd_h.ap()[2 * q], csm4[q % 4][0:16, :], ("cdpa", q))
                dma(f"gc{q % 4}", cd_h.ap()[2 * q + 1], csm4[q % 4][16:32, :], ("cdp", q))
            for h in range(2):
                st.wait(en, "v", ("rm", "own", h))
                dma("qm", mr_h.ap()[0:1, h * 1024:(h + 1) * 1024], row_sb[:, :], ("smr", h))
                st.wait(en, "v", ("rr", "own", h))
                dma("qm", mr_h.ap()[1:2, h * 1024:(h + 1) * 1024], row_sb[:, :], ("srr", "own", h))
            for q in range(4, 32):
                st.wait(en, "s", ("cevp", q))
                dma(f"gc{q % 4}", cd_h.ap()[2 * q], csm4[q % 4][0:16, :], ("cdpa", q))
                dma(f"gc{q % 4}", cd_h.ap()[2 * q + 1], csm4[q % 4][16:32, :], ("cdp", q))
            for h in range(2):
                st.wait(en, "v", ("rr", "sib", h))
                dma("qm", mr_h.ap()[2:3, h * 1024:(h + 1) * 1024], row_sb[:, :], ("srr", "sib", h))
            for i in range(16):
                h, ft = divmod(i, 8)
                if i == 0:
                    st.wait(en, "s", ("sbq", 15))
                    st.wait(en, "t", ("smm", 15))
                elif i == 1:
                    st.wait(en, "t", ("sqc", 15))
                else:
                    st.wait(en, "v", ("xsb", i - 2))
                dma(f"qx{i % 2}", p2[i % 2][:, :],
                    xst_h.ap()[ft * P:(ft + 1) * P, h * 1024:(h + 1) * 1024], ("sc2", i))
            st.wait(en, "t", ("tavrow",))
            for rg in range(4):
                if rg >= 2:
                    st.wait(en, "v", ("xadd", 4 * (rg - 2) + 3))
                dma(f"qx{rg % 2}",
                    xr_sb[:, (rg % 2) * 4096:(rg % 2) * 4096 + 4096].rearrange("p (b f) -> p b f", b=4),
                    xo_h.ap()[rg * 512:(rg + 1) * 512, :].rearrange("(b p) f -> p b f", p=P),
                    ("fx", rg))
            for og in range(4):
                st.wait(en, "v", ("spill", NG - 1, 4 * og + 3))
                dma("qo", out_h.ap()[og * 512:(og + 1) * 512, :].rearrange("(b p) f -> p b f", p=P),
                    out_acc[:, og * 4096:(og + 1) * 4096].rearrange("p (b f) -> p b f", b=4),
                    ("ox", og))
            st.wait(en, "qo", 64)

        def build_g(st, en):
            def dma(sem, dst, src, key):
                st.emit(lambda: en.dma_start(dst, src), sem=sem, key=key, n=16)

            dma("gx0", xn_t[:, 0:2048], xt_h.ap()[0:P, :], ("xT", 0))
            dma("gu", mt_s, mk_h.ap(), ("mk",))
            dma("gu", ut_s, up_h.ap(), ("up",))
            for ft in range(1, NF):
                if ft >= 2:
                    st.wait(en, f"gx{ft % 2}", ("xT", ft - 2))
                dma(f"gx{ft % 2}", xn_t[:, ft * 2048:(ft + 1) * 2048],
                    xt_h.ap()[ft * P:(ft + 1) * P, :], ("xT", ft))
            def sib_p1(i0, i1):
                for i in range(i0, i1):
                    h, ft = divmod(i, 8)
                    if i >= 4:
                        st.wait(en, "s", ("sbq", i - 4))
                        st.wait(en, "t", ("smm", i - 4))
                    dma(f"sb{i % 4}", sib4[i % 4][:, :],
                        xst_h.ap()[ft * P:(ft + 1) * P, h * 1024:(h + 1) * 1024], ("sc1", i))

            sib_p1(0, 4)
            st.wait(en, "qm", ("srr", "own", 1))
            st.emit(lambda: en.dma_start(
                mb_bc[:, :], bass.AP(mr_h, 0, [[0, P], [1, TOK]])), sem="gb", key=("mb",), n=16)
            st.emit(lambda: en.dma_start(
                rb_bc[:, :], bass.AP(mr_h, TOK, [[0, P], [1, TOK]])), sem="gb", key=("rb",), n=16)

            def load_wc(g):
                for hh in range(2):
                    src = wc_h.ap()[g * 1024 + hh * 512: g * 1024 + (hh + 1) * 512, :]
                    dma(f"gw{g % 2}",
                        wc_bf[g % 2][:, hh * 4096:(hh + 1) * 4096].rearrange("p (l of) -> p l of", l=4),
                        src.rearrange("(l p) of -> p l of", p=P), ("wc", g, hh))

            def load_cg(g):
                st.wait(en, f"gc{(2 * g) % 4}", ("cdp", 2 * g))
                st.wait(en, f"gc{(2 * g + 1) % 4}", ("cdp", 2 * g + 1))
                st.emit(lambda: en.memset(cg[g % 2][:, :], 0.0), sem="ge", key=("ms", g))
                st.wait(en, "ge", ("ms", g))
                for rl in range(4):
                    r = 4 * g + rl
                    rr = r % 8
                    dma(f"gs{g % 2}", cg[g % 2][16 * rr: 16 * rr + 16, rl * 256:(rl + 1) * 256],
                        cd_h.ap()[r], ("cs", g, rl))

            load_cg(0)
            load_cg(1)
            load_wc(0)
            sib_p1(4, 16)
            st.wait(en, "qm", ("srr", "sib", 1))
            st.emit(lambda: en.dma_start(
                sib_rb[:, :], bass.AP(mr_h, 2 * TOK, [[0, P], [1, TOK]])), sem="gb2", key=("sb_rb",), n=16)
            st.wait(en, "t", ("cmm", 63))
            load_wc(1)

            def load_attn_w(which):
                st.wait(en, "v", ("xsb", 15))
                st.wait(en, "t", ("sqc", 15))
                if which == "wv":
                    for half in range(2):
                        dma("qwv", wv_bf[:, half * 4096:(half + 1) * 4096].rearrange("p (c l) -> p c l", c=4),
                            wv_h.ap()[half * 512:(half + 1) * 512, :].rearrange("(c p) l -> p c l", p=P),
                            ("wv", half))
                else:
                    for qq in range(28, 32):
                        st.wait(en, f"gc{qq % 4}", ("cdp", qq))
                    for half in range(2):
                        dma("qwo", wo_bf[:, half * 4096:(half + 1) * 4096].rearrange("p (c l) -> p c l", c=4),
                            wo_h.ap()[half * 512:(half + 1) * 512, :].rearrange("(c p) l -> p c l", p=P),
                            ("wo", half))

            for g in range(2, NG):
                st.wait(en, "t", ("cp", g - 2, NT - 1))
                load_wc(g)
                st.wait(en, "t", ("z", g - 2, 15))
                load_cg(g)
                if g == 2:
                    load_attn_w("wv")
                elif g == 3:
                    load_attn_w("wo")

        def build_t(st, en):
            st.wait(en, "gu", ("up",))
            st.wait(en, "v", ("init",))
            rail_i = [8]

            def rail_fill(n):
                r1 = min(64, rail_i[0] + n)
                if r1 > rail_i[0]:
                    c_rails(rail_i[0], r1)
                    rail_i[0] = r1

            def own_stats():
              for h in range(2):
                for ft in range(NF):
                    st.wait(en, f"gx{ft % 2}", ("xT", ft))
                    if h == 1 and ft == 0:
                        st.wait(en, "v", ("rm", "own", 0))
                    for nn in range(2):
                        last = ft == 7 and nn == 1
                        st.emit(lambda h=h, ft=ft, nn=nn: en.matmul(
                            rowm[h][0:1, nn * 512:(nn + 1) * 512], ones_b[:, 0:1],
                            xn_t[:, ft * 2048 + h * 1024 + nn * 512: ft * 2048 + h * 1024 + (nn + 1) * 512],
                            start=(ft == 0), stop=(ft == 7)),
                            sem="t" if last else None, key=("mmh", h) if last else None)
              for h in range(2):
                for ft in range(NF):
                    i = h * 8 + ft
                    st.wait(en, "s", ("sq", i))
                    for nn in range(2):
                        st.emit(lambda h=h, i=i, nn=nn: en.matmul(
                            rowq[h][0:1, nn * 512:(nn + 1) * 512], ones_b[:, 0:1],
                            h_t[:, i * 1024 + nn * 512: i * 1024 + (nn + 1) * 512],
                            start=(i % 8 == 0), stop=(i % 8 == 7)),
                            sem="t" if nn == 1 else None, key=("qc", i) if nn == 1 else None)

            def c_rails(r0, r1):
                for q in range(r0 // 2, r1 // 2):
                    if q >= 2:
                        st.wait(en, "s", ("cevp", q - 2))
                    for hh in range(4):
                        last = hh == 3
                        st.emit(lambda hh=hh, q=q: en.matmul(
                            ps_C[q % 2][:, hh * 64:(hh + 1) * 64],
                            mt_s[:, q * 128 + hh * 32: q * 128 + (hh + 1) * 32],
                            ut_s[:, q * 256 + hh * 64: q * 256 + (hh + 1) * 64],
                            start=True, stop=True),
                            sem="t" if last else None, key=("cmm", 2 * q + 1) if last else None)

            def sib_mm(i):
                h = i // 8
                st.wait(en, f"sb{i % 4}", ("sc1", i))
                if i in (0, 8):
                    st.wait(en, "s", ("cevp", 31))
                for nn in range(2):
                    st.emit(lambda h=h, i=i, nn=nn: en.matmul(
                        s_rowm4[h][0:1, nn * 512:(nn + 1) * 512], ones_b[:, 0:1],
                        sib4[i % 4][:, nn * 512:(nn + 1) * 512],
                        start=(i % 8 == 0), stop=(i % 8 == 7)),
                        sem="t" if nn == 1 else None, key=("smm", i) if nn == 1 else None)

            def sib_sq(i):
                h = i // 8
                st.wait(en, "s", ("sbq", i))
                if i == 8:
                    st.wait(en, "v", ("s2p", 0))
                for nn in range(2):
                    st.emit(lambda h=h, i=i, nn=nn: en.matmul(
                        s_rowq4[h][0:1, nn * 512:(nn + 1) * 512], ones_b[:, 0:1],
                        x2b[i % 2][:, nn * 512:(nn + 1) * 512],
                        start=(i % 8 == 0), stop=(i % 8 == 7)),
                        sem="t" if nn == 1 else None, key=("sqc", i) if nn == 1 else None)

            c_rails(0, 8)
            own_stats()
            rail_fill(64)
            def z_slot(g, i):
                return ps_z[0]

            def z_group(g, i):
                ft = g // 2
                tch, mc = divmod(i, 8)
                if i == 0:
                    st.wait(en, f"gs{g % 2}", ("cs", g, 3))
                    st.wait(en, "v", ("xnt", ft))
                if g == 0:
                    if i >= 1:
                        st.wait(en, "s", ("gelu", 0, i - 1))
                    else:
                        st.wait(en, "v", ("rm", "own", 1))
                elif i == 0:
                    st.wait(en, "s", ("gelu", g - 1, 15))
                else:
                    st.wait(en, "s", ("gelu", g, i - 1))
                for h2 in range(2):
                    last = h2 == 1
                    st.emit(lambda g=g, i=i, tch=tch, mc=mc, h2=h2, ft=ft: en.matmul(
                        z_slot(g, i)[:, h2 * 512:(h2 + 1) * 512],
                        cg[g % 2][:, mc * P:(mc + 1) * P],
                        xn_t[:, ft * 2048 + tch * 1024 + h2 * 512: ft * 2048 + tch * 1024 + (h2 + 1) * 512],
                        start=True, stop=True),
                        sem="t" if last else None, key=("z", g, i) if last else None)

            def attn_block():
                st.wait(en, "v", ("s2r",))
                st.emit(lambda: en.matmul(ps_z[1][0:P, 640:641], ones_rc[0:1, 0:P],
                                          s2a[0:1, 4:5], start=True, stop=True),
                        sem="t", key=("s2b",))
                st.wait(en, "qwv", ("wv", 1))
                st.wait(en, "qwo", ("wo", 1))
                st.wait(en, "v", ("xsbf",))
                for fc in range(8):
                    for half in range(2):
                        last = fc == 7 and half == 1
                        st.emit(lambda fc=fc, half=half: en.matmul(
                            ps_z[1][0:1, half * 512:(half + 1) * 512],
                            xsum_bf[:, fc:fc + 1],
                            wv_bf[:, fc * 1024 + half * 512: fc * 1024 + (half + 1) * 512],
                            start=(fc == 0), stop=(fc == 7)),
                            sem="t" if last else None, key=("tvrow",) if last else None)
                st.wait(en, "v", ("ev_vrow",))
                for j in range(8):
                    st.emit(lambda j=j: en.matmul(
                        ps_z[1][0:P, 512 + j:512 + j + 1], vsum_row[0:1, j * P:(j + 1) * P],
                        ones_rc[0:1, 0:1], start=True, stop=True),
                        sem="t" if j == 7 else None, key=("t8v",) if j == 7 else None)
                st.wait(en, "v", ("ev_vbf",))
                for fc in range(8):
                    for half in range(2):
                        last = fc == 7 and half == 1
                        st.emit(lambda fc=fc, half=half: en.matmul(
                            ps_z[1][0:1, half * 512:(half + 1) * 512],
                            vsum_bf[:, fc:fc + 1],
                            wo_bf[:, fc * 1024 + half * 512: fc * 1024 + (half + 1) * 512],
                            start=(fc == 0), stop=(fc == 7)),
                            sem="t" if last else None, key=("tavrow",) if last else None)
                st.wait(en, "v", ("ev_avrow",))
                for half in range(2):
                    st.emit(lambda half=half: en.matmul(
                        ps_z[1][:, half * 512:(half + 1) * 512],
                        ones_rc[0:1, 0:P], attn_row[0:1, half * 512:(half + 1) * 512],
                        start=True, stop=True),
                        sem="t" if half == 1 else None, key=("tbc",) if half == 1 else None)

            def cp_tile(g, tt):
                if tt == 0:
                    st.wait(en, "s", ("gelu", g, 7))
                if tt == 8:
                    st.wait(en, "s", ("gelu", g, 15))
                si = 16 * g + tt - 2
                if si >= 0:
                    st.wait(en, "v", ("spill", si // 16, si % 16))
                for ht in range(8):
                    for half in range(2):
                        last = ht == 7 and half == 1
                        st.emit(lambda g=g, tt=tt, ht=ht, half=half: en.matmul(
                            ps_o[tt % 2][:, half * 512:(half + 1) * 512],
                            h_t[:, ht * 2048 + tt * P: ht * 2048 + (tt + 1) * P],
                            wc_bf[g % 2][:, ht * 1024 + half * 512: ht * 1024 + (half + 1) * 512],
                            start=(ht == 0), stop=(ht == 7)),
                            sem="t" if last else None, key=("cp", g, tt) if last else None)

            st.wait(en, "gw0", ("wc", 0, 1))
            for i in range(16):
                z_group(0, i)
                sib_mm(i)
                if i < 8:
                    sib_sq(i)
                elif i >= 10:
                    sib_sq(i - 2)
                if i >= 8:
                    cp_tile(0, i - 8)
            for g in range(NG):
                if g == 3:
                    attn_block()
                if g > 0:
                    st.wait(en, f"gw{g % 2}", ("wc", g, 1))
                for tt in (range(8, NT) if g == 0 else range(NT)):
                    if tt < 8 and g > 0:
                        z_group(g, 8 + tt)
                    if tt >= 8 and g + 1 < NG:
                        z_group(g + 1, tt - 8)
                    if g == 0 and tt in (8, 9):
                        sib_sq(tt + 6)
                    cp_tile(g, tt)

        def build_s(st, en):
            st.bind_self(en, "s")

            def squares(kind, h, fill=None):
                for ft in range(NF):
                    i = h * 8 + ft
                    if kind == "own":
                        st.wait(en, f"gx{ft % 2}", ("xT", ft))
                        st.emit(lambda h=h, ft=ft, i=i: en.activation(
                            h_t[:, i * 1024:(i + 1) * 1024],
                            xn_t[:, ft * 2048 + h * 1024: ft * 2048 + (h + 1) * 1024],
                            AF.Square), sem="s", key=("sq", i))
                        if fill is not None:
                            fill(2)
                    else:
                        st.wait(en, f"sb{i % 4}", ("sc1", i))
                        if i >= 2:
                            st.wait(en, "t", ("sqc", i - 2))
                        else:
                            st.wait(en, "t", ("qc", 14 + i))
                        st.emit(lambda i=i: en.activation(
                            x2b[i % 2][:, :], sib4[i % 4][:, :],
                            AF.Square), sem="s", key=("sbq", i))

            def sbq_one(i):
                st.wait(en, f"sb{i % 4}", ("sc1", i))
                if i >= 2:
                    st.wait(en, "t", ("sqc", i - 2))
                else:
                    st.wait(en, "t", ("qc", 14 + i))
                st.emit(lambda i=i: en.activation(
                    x2b[i % 2][:, :], sib4[i % 4][:, :],
                    AF.Square), sem="s", key=("sbq", i))

            def sqrt_seg(kind, h, q_ps):
                st.wait(en, "v", ("var", kind, h))
                st.emit(lambda q_ps=q_ps: en.activation(
                    q_ps, q_ps, AF.Sqrt, bias=eps_c[0:1, 0:1]),
                    sem="s", key=("rsq", kind, h))

            def cevps(q0, q1):
                for q in range(q0, q1):
                    st.wait(en, "t", ("cmm", 2 * q + 1))
                    if q >= 4:
                        st.wait(en, f"gc{q % 4}", ("cdp", q - 4))
                    st.emit(lambda q=q: en.activation(
                        csm4[q % 4][:, :], ps_C[q % 2][:, :], AF.Identity),
                        sem="s", key=("cevp", q))

            cevps(0, 4)
            squares("own", 0)
            squares("own", 1)
            sqrt_seg("own", 0, rowq[0])
            sqrt_seg("own", 1, rowq[1])
            cevps(4, 32)
            for g in range(NG):
                for i in range(16):
                    tch, mc = divmod(i, 8)
                    st.wait(en, "t", ("z", g, i))
                    st.emit(lambda g=g, i=i, tch=tch, mc=mc: en.activation(
                        h_t[:, mc * 2048 + tch * 1024: mc * 2048 + (tch + 1) * 1024],
                        ps_z[0][:, :], AF.Gelu),
                        sem="s", key=("gelu", g, i))
                    if g == 0:
                        sbq_one(i)
                    if g == 0 and i == 9:
                        sqrt_seg("sib", 0, s_rowq4[0])
                    if g == 1 and i == 1:
                        sqrt_seg("sib", 1, s_rowq4[1])

        def build_v(st, en):
            st.bind_self(en, "v")

            def vself():
                st.wait(en, "v", st.cnt["v"] if not st.record else 0)

            st.emit(lambda: en.memset(ones_rc[:, :], 1.0))
            st.emit(lambda: en.memset(ones_b[:, :], 1.0))
            st.emit(lambda: en.memset(eps_c[:, :], EPS), sem="v", key=("init",))

            def seg(kind, h, m_ps, q_ps):
                if kind == "own" and h == 1:
                    st.wait(en, "qm", ("srr", "own", 0))
                if kind == "sib":
                    st.wait(en, "qm", ("srr", "own", 1) if h == 0 else ("srr", "sib", 0))
                st.wait(en, "t", ("mmh", h) if kind == "own" else ("smm", h * 8 + 7))
                st.emit(lambda m_ps=m_ps: en.tensor_scalar(
                    row_sb[:, :], m_ps, 1.0 / D, None, ALU.mult),
                    sem="v", key=("rm", kind, h))
                vself()
                st.emit(lambda: en.tensor_tensor(
                    attn_row[0:1, :], row_sb[:, :], row_sb[:, :], ALU.mult), sem="v")
                st.wait(en, "t", ("qc", h * 8 + 7) if kind == "own" else ("sqc", h * 8 + 7))
                vself()
                st.emit(lambda q_ps=q_ps: en.tensor_scalar(
                    q_ps, q_ps, 1.0 / D, None, ALU.mult), sem="v")
                vself()
                st.emit(lambda q_ps=q_ps: en.tensor_tensor(
                    q_ps, q_ps, attn_row[0:1, :], ALU.subtract), sem="v", key=("var", kind, h))
                st.wait(en, "s", ("rsq", kind, h))
                if kind == "own":
                    st.wait(en, "qm", ("smr", h))
                    st.emit(lambda q_ps=q_ps: en.tensor_copy(row_sb[:, :], q_ps),
                            sem="v", key=("rr", "own", h))
                else:
                    st.emit(lambda q_ps=q_ps, m_ps=m_ps: en.reciprocal(m_ps, q_ps), sem="v")
                    vself()
                    st.emit(lambda q_ps=q_ps, m_ps=m_ps: en.tensor_tensor(
                        q_ps, row_sb[:, :], m_ps, ALU.mult), sem="v")
                    vself()
                    st.emit(lambda q_ps=q_ps, h=h: en.reduce_sum(
                        s2a[0:1, h:h + 1], q_ps, axis=AX.X), sem="v", key=("s2p", h))
                    vself()
                    st.emit(lambda m_ps=m_ps: en.tensor_copy(row_sb[:, :], m_ps),
                            sem="v", key=("rr", "sib", h))

            seg("own", 0, rowm[0], rowq[0])
            seg("own", 1, rowm[1], rowq[1])
            st.wait(en, "gb", ("rb",))
            def _rbr():
                with nc.allow_low_precision(reason="rstd broadcast was already bf16"):
                    return en.reciprocal(rb_bc[:, :], rb_bc[:, :])
            st.emit(_rbr, sem="v", key=("rbr",))

            def norm(f0, f1):
                for ft in range(f0, f1):
                    st.wait(en, f"gx{ft % 2}", ("xT", ft))
                    st.emit(lambda ft=ft: en.tensor_tensor(
                        xn_t[:, ft * 2048:(ft + 1) * 2048], xn_t[:, ft * 2048:(ft + 1) * 2048],
                        mb_bc[:, :], ALU.subtract), sem="v")
                    st.emit(lambda ft=ft: en.tensor_tensor(
                        xn_t[:, ft * 2048:(ft + 1) * 2048], xn_t[:, ft * 2048:(ft + 1) * 2048],
                        rb_bc[:, :], ALU.mult), sem="v", key=("xnt", ft))
                    st.emit(lambda ft=ft: en.reduce_sum(
                        xsum_t[:, ft:ft + 1], xn_t[:, ft * 2048:(ft + 1) * 2048],
                        axis=AX.X), sem="v", key=("xso", ft))

            norm(0, 1)

            def p2_item(i):
                h, ft = divmod(i, 8)
                if i == 0:
                    st.wait(en, "gb2", ("sb_rb",))
                st.wait(en, f"qx{i % 2}", ("sc2", i))
                st.emit(lambda i=i, h=h: en.tensor_tensor(
                    p2[i % 2], p2[i % 2], sib_rb[:, h * 1024:(h + 1) * 1024],
                    ALU.mult), sem="v")
                vself()
                st.emit(lambda i=i: en.reduce_sum(
                    xsb[:, i:i + 1], p2[i % 2], axis=AX.X), sem="v", key=("xsb", i))

            def asm_item():
                vself()
                st.emit(lambda: en.tensor_tensor(xsum_t[:, :], xsum_t[:, :], xsb[:, 0:8], ALU.add), sem="v")
                vself()
                st.emit(lambda: en.tensor_tensor(xsum_t[:, :], xsum_t[:, :], xsb[:, 8:16], ALU.add), sem="v")
                st.emit(lambda: en.tensor_tensor(s2a[0:1, 4:5], s2a[0:1, 0:1], s2a[0:1, 1:2], ALU.add),
                        sem="v", key=("s2r",))

    
            def seg_sib(h):
                return lambda: seg("sib", h, s_rowm4[h], s_rowq4[h])
            norms = [(lambda ft=ft: norm(ft, ft + 1)) for ft in range(1, NF)]
            defer = [seg_sib(0), norms[0], norms[1], norms[2], seg_sib(1)]
            defer += norms[3:]
            defer += [(lambda i=i: p2_item(i)) for i in range(16)]
            defer.append(asm_item)

            def attn_evacs():
                st.wait(en, "t", ("s2b",))
                st.emit(lambda: en.tensor_copy(s2c[:, :], ps_z[1][0:P, 640:641]), sem="v", key=("s2c",))
                vself()
                st.emit(lambda: en.tensor_scalar(
                    xsum_t[:, :], xsum_t[:, :], s2c[:, 0:1], None, ALU.subtract),
                    sem="v", key=("xsc",))
                vself()
                st.emit(lambda: en.tensor_copy(xsum_bf[:, :], xsum_t[:, :]),
                        sem="v", key=("xsbf",))
                st.wait(en, "t", ("tvrow",))
                st.emit(lambda: en.tensor_copy(vsum_row[:, :], ps_z[1][0:1, 0:D]), sem="v", key=("ev_vrow",))
                st.wait(en, "t", ("t8v",))
                st.emit(lambda: en.tensor_copy(vsum_t[:, :], ps_z[1][0:P, 512:520]), sem="v", key=("ev_v",))
                vself()
                st.emit(lambda: en.tensor_copy(vsum_bf[:, :], vsum_t[:, :]),
                        sem="v", key=("ev_vbf",))
                st.wait(en, "t", ("tavrow",))
                st.emit(lambda: en.tensor_copy(attn_row[:, :], ps_z[1][0:1, 0:D]), sem="v", key=("ev_avrow",))

            st.wait(en, "t", ("cmm", 63))
            slot = 0
            for g in range(NG):
                if g == 3:
                    attn_evacs()
                for tt in range(NT):
                    st.wait(en, "t", ("cp", g, tt))
                    dst = out_acc[:, tt * 1024:(tt + 1) * 1024]
                    if g == 0:
                        st.emit(lambda dst=dst, tt=tt: en.tensor_copy(dst, ps_o[tt % 2][:, :]),
                                sem="v", key=("spill", g, tt))
                    else:
                        st.emit(lambda dst=dst, tt=tt: en.tensor_tensor(
                            dst, dst, ps_o[tt % 2][:, :], ALU.add),
                            sem="v", key=("spill", g, tt))
                    if defer and (slot >= 14 or slot % 2 == 0):
                        defer.pop(0)()
                    slot += 1
                    if g == 5 and tt < 8 or g == 6 and tt >= 8:
                        rg = tt // 4
                        st.wait(en, f"qx{rg % 2}", ("fx", rg))
                        xsrc = xr_sb[:, (rg % 2) * 4096 + (tt % 4) * 1024: (rg % 2) * 4096 + (tt % 4 + 1) * 1024]
                        st.emit(lambda dst=dst, xsrc=xsrc: en.tensor_tensor(
                            dst, dst, xsrc, ALU.add), sem="v", key=("xadd", tt))
                    if g == 8:
                        if tt == 0:
                            st.wait(en, "t", ("tbc",))
                        st.emit(lambda dst=dst: en.tensor_tensor(
                            dst, dst, ps_z[1][:, 0:D], ALU.add), sem="v", key=("aadd", tt))

        for b_ in (build_q, build_g, build_t, build_s, build_v):
            b_(St(True, sems, EV), None)
        with nc.Block() as block:
            @block.sync
            def _(sync):
                build_q(St(False, sems, EV), sync)

            @block.gpsimd
            def _(gpsimd):
                build_g(St(False, sems, EV), gpsimd)

            @block.tensor
            def _(tensor):
                build_t(St(False, sems, EV), tensor)

            @block.scalar
            def _(scalar):
                build_s(St(False, sems, EV), scalar)

            @block.vector
            def _(vector):
                build_v(St(False, sems, EV), vector)

    return nc


def kernel(L0_attn_x, L0_attn_w_qkv, L0_attn_w_o, L0_ffn_rail_maker,
           L0_ffn_rail_up_proj, L0_ffn_w_cproj):
    X = np.ascontiguousarray(np.asarray(L0_attn_x, dtype=np.float32)).reshape(16384, 1024)
    wqkv = np.asarray(L0_attn_w_qkv, dtype=np.float32)
    wv_t = np.ascontiguousarray(wqkv[2048:3072, :].T)
    wo_t = np.ascontiguousarray(np.asarray(L0_attn_w_o, dtype=np.float32).T)
    mkju = np.asarray(L0_ffn_rail_maker, dtype=np.float32).transpose(0, 1, 3, 2)
    upju = np.asarray(L0_ffn_rail_up_proj, dtype=np.float32).transpose(0, 1, 3, 2)
    mk2 = np.zeros((32, 32, 4, 32), dtype=np.float32)
    up2 = np.empty((32, 32, 4, 64), dtype=np.float32)
    for rl in range(2):
        mk2[16 * rl:16 * rl + 16, :, :, 16 * rl:16 * rl + 16] = mkju[:, rl::2].transpose(2, 1, 0, 3)
        up2[16 * rl:16 * rl + 16] = upju[:, rl::2].transpose(2, 1, 0, 3)
    mk_t = np.ascontiguousarray(mk2.reshape(32, 4096))
    up_t = np.ascontiguousarray(up2.reshape(32, 8192))
    wc = np.asarray(L0_ffn_w_cproj, dtype=np.float32)
    wc_t = np.ascontiguousarray(
        wc.reshape(1024, 4, 64, 64).transpose(2, 1, 3, 0).reshape(16384, 1024))

    XT = np.ascontiguousarray(X.T)

    nc = build_nc()
    in_maps = []
    for core in range(8):
        sib_ = core ^ 1
        in_maps.append({
            "xo": X[core * TOK:(core + 1) * TOK],
            "xT": np.ascontiguousarray(XT[:, core * TOK:(core + 1) * TOK]),
            "xsT": np.ascontiguousarray(XT[:, sib_ * TOK:(sib_ + 1) * TOK]),
            "wv_t": wv_t, "wo_t": wo_t, "mk_t": mk_t, "up_t": up_t,
            "wc_t": wc_t,
        })
    res = run_bass_kernel_spmd(nc, in_maps, core_ids=list(range(8)))
    out = np.concatenate([np.asarray(res.results[c]["out"]) for c in range(8)], axis=0)
    return out.reshape(4, 4096, 1024).astype(np.float32)


if __name__ == "__main__":
    rng = np.random.default_rng(0)
    ins = {
        "L0_attn_x": rng.standard_normal((4, 4096, 1024), dtype=np.float32),
        "L0_attn_w_qkv": (rng.standard_normal((3072, 1024), dtype=np.float32) * 0.02),
        "L0_attn_w_o": (rng.standard_normal((1024, 1024), dtype=np.float32) * 0.02),
        "L0_ffn_rail_maker": (rng.standard_normal((4, 64, 16, 16), dtype=np.float32) * 0.02),
        "L0_ffn_rail_up_proj": (rng.standard_normal((4, 64, 64, 16), dtype=np.float32) * 0.02),
        "L0_ffn_w_cproj": (rng.standard_normal((1024, 16384), dtype=np.float32) * 0.02),
    }
    o = kernel(**ins)
    print("out", o.shape, o.dtype, np.abs(o).max())
